# version 03
# baseline (speedup 1.0000x reference)
import contextlib
from dataclasses import dataclass, field

import numpy as np
import ml_dtypes

import concourse.bass as bass
import concourse.mybir as mybir
from concourse.bass_utils import run_bass_kernel_spmd

F32 = mybir.dt.float32
BF16 = mybir.dt.bfloat16
AF = mybir.ActivationFunctionType
ALU = mybir.AluOpType
AX = mybir.AxisListType

NCORES = 8
H = 4096
DIN = 8192
NST = 16
L = 16384
TC = L // NCORES
CH = DIN // NCORES
EPS_LN = 1e-5
EPS_RMS = 1e-6
BF = ml_dtypes.bfloat16

ENGINES = ("sync", "vector", "scalar", "tensor", "gpsimd")


@dataclass
class _Op:
    eng: str
    fn: object
    inc_sem: object
    inc_amt: int
    waits: list = field(default_factory=list)


class _Group:
    def __init__(self, prog, eng):
        self.prog, self.eng = prog, eng
        self.items = []

    def op(self, fn, reads=(), writes=()):
        self.items.append((fn, list(reads), list(writes)))

    def __enter__(self):
        return self

    def __exit__(self, *exc):
        if exc[0] is None:
            self.prog._emit_group(self.eng, self.items)
        return False


class Prog:

    def __init__(self, nc):
        self.nc = nc
        self.ops = []
        self.n_eng = {e: 0 for e in ENGINES}
        self.n_ch = {}
        self.ch_eng = {}
        self.last_w = {}
        self.readers = {}
        self.waited = {e: {} for e in ENGINES}
        self._sems_used = set()

    def _edge(self, eng, sem_key, value):
        if sem_key == ("e", "tensor") and eng == "tensor":
            return None
        w = self.waited[eng]
        if w.get(sem_key, -1) >= value:
            return None
        w[sem_key] = value
        self._sems_used.add(sem_key)
        return (sem_key, value)

    def _collect_waits(self, eng, reads, writes):
        waits = []
        for k in reads:
            lw = self.last_w.get(k)
            if lw is not None:
                e = self._edge(eng, lw[0], lw[1])
                if e:
                    waits.append(e)
        for k in writes:
            lw = self.last_w.get(k)
            if lw is not None:
                e = self._edge(eng, lw[0], lw[1])
                if e:
                    waits.append(e)
            for r in self.readers.get(k, ()):
                e = self._edge(eng, r[0], r[1])
                if e:
                    waits.append(e)
        return waits

    def _record(self, reads, writes, done):
        for k in reads:
            self.readers.setdefault(k, []).append(done)
        for k in writes:
            self.last_w[k] = done
            self.readers[k] = []

    def _emit_group(self, eng, items):
        all_reads, all_writes, waits = [], [], []
        seen_w = set()
        for fn, reads, writes in items:
            w = self._collect_waits(eng, reads,
                                    [k for k in writes if k not in seen_w])
            waits.extend(w)
            seen_w.update(writes)
            all_reads.extend(reads)
            all_writes.extend(writes)
        self.n_eng[eng] += len(items)
        inc_sem = ("e", eng)
        self._sems_used.add(inc_sem)
        done = (inc_sem, self.n_eng[eng])
        self._record(all_reads, all_writes, done)
        for i, (fn, reads, writes) in enumerate(items):
            self.ops.append(_Op(eng=eng, fn=fn,
                                inc_sem=inc_sem if i == len(items) - 1 else None,
                                inc_amt=len(items) if i == len(items) - 1 else 0,
                                waits=waits if i == 0 else []))

    def group(self, eng):
        return _Group(self, eng)

    def op(self, eng, fn, reads=(), writes=()):
        waits = self._collect_waits(eng, list(reads), list(writes))
        self.n_eng[eng] += 1
        inc_sem = ("e", eng)
        self._sems_used.add(inc_sem)
        done = (inc_sem, self.n_eng[eng])
        self._record(list(reads), list(writes), done)
        self.ops.append(_Op(eng=eng, fn=fn, inc_sem=inc_sem, inc_amt=1,
                            waits=waits))
        return done

    def dma(self, eng, out, in_, ch, reads=(), writes=(), **kw):
        fn = lambda E, out=out, in_=in_, kw=kw: E.dma_start(out=out, in_=in_, **kw)
        waits = self._collect_waits(eng, list(reads), list(writes))
        self.n_ch[ch] = self.n_ch.get(ch, 0) + 1
        self.ch_eng[ch] = eng
        inc_sem = ("c", ch)
        self._sems_used.add(inc_sem)
        done = (inc_sem, 16 * self.n_ch[ch])
        self._record(list(reads), list(writes), done)
        self.ops.append(_Op(eng=eng, fn=fn, inc_sem=inc_sem, inc_amt=16,
                            waits=waits))
        return done

    def emit(self):
        nc = self.nc
        with contextlib.ExitStack() as es:
            sems = {sk: es.enter_context(nc.semaphore(f"{sk[0]}_{sk[1]}"))
                    for sk in sorted(self._sems_used, key=str)}
            block = es.enter_context(nc.Block())
            per_eng = {e: [op for op in self.ops if op.eng == e]
                       for e in ENGINES}

            def mk_body(ename):
                ops = per_eng[ename]

                def body(E):
                    for op in ops:
                        for sem_key, value in op.waits:
                            E.wait_ge(sems[sem_key], value)
                        instr = op.fn(E)
                        if instr is not None and op.inc_sem is not None:
                            instr.then_inc(sems[op.inc_sem], op.inc_amt)
                    for ch, cnt in self.n_ch.items():
                        if self.ch_eng[ch] == ename:
                            E.wait_ge(sems[("c", ch)], 16 * cnt)
                    if ("e", ename) in sems and self.n_eng[ename] > 0:
                        E.wait_ge(sems[("e", ename)], self.n_eng[ename])
                return body

            for ename in ENGINES:
                if per_eng[ename]:
                    getattr(block, ename)(mk_body(ename))


def _c(a):
    return np.ascontiguousarray(a)


class Ctx:
    pass


L0_ROWS, L0_COLS = 2048, 8192


def build_L0():
    nc = bass.Bass("TRN2", target_bir_lowering=False)
    w_in = nc.dram_tensor("w_in", [L0_ROWS, L0_COLS], F32, kind="ExternalInput").ap()
    w_out = nc.dram_tensor("w_out", [L0_ROWS, L0_COLS], BF16, kind="ExternalOutput").ap()
    with contextlib.ExitStack() as es:
        a32 = [es.enter_context(nc.sbuf_tensor(f"a32_{i}", [128, L0_COLS], F32))
               for i in range(2)]
        a16 = [es.enter_context(nc.sbuf_tensor(f"a16_{i}", [128, L0_COLS], BF16))
               for i in range(2)]
        p = Prog(nc)
        for k in range(L0_ROWS // 128):
            s = k % 2
            p.dma("sync", a32[s][:], w_in[k * 128:(k + 1) * 128, :],
                  ch=f"ld_{s}", writes=[("a32", s)])
            p.op("scalar", lambda E, s=s: E.activation(
                out=a16[s][:], in_=a32[s][:], func=AF.Copy),
                reads=[("a32", s)], writes=[("a16", s)])
            p.dma("gpsimd", w_out[k * 128:(k + 1) * 128, :], a16[s][:],
                  ch=f"st_{s}", reads=[("a16", s)])
        p.emit()
    return nc


def _pack_weights_fp32(inputs):
    mats = {
        "inprojT": np.asarray(inputs["L0_attn_in_proj_w"], np.float32),
        "outprojT": _c(np.asarray(inputs["L0_attn_out_proj_w"], np.float32).T),
        "baseT": _c(np.asarray(inputs["L0_ffn_W_base"], np.float32).T),
        "xpT": _c(np.asarray(inputs["L0_attn_x_proj_w"], np.float32).T),
        "dtpT": _c(np.asarray(inputs["L0_attn_dt_proj_w"], np.float32).T),
        "afT": _c(np.asarray(inputs["L0_ffn_A"], np.float32).reshape(128, H).T),
        "bflat": _c(np.asarray(inputs["L0_ffn_B"], np.float32)
                    .transpose(0, 2, 1).reshape(128, H)),
        "rtT": _c(np.asarray(inputs["L0_ffn_W_router"], np.float32).T),
    }
    cap = NCORES * L0_ROWS * L0_COLS
    total = sum(m.size for m in mats.values())
    assert total <= cap, (total, cap)
    flat = np.zeros(cap, np.float32)
    info = {}
    off = 0
    for k, m in mats.items():
        info[k] = (off, m.shape)
        flat[off:off + m.size] = m.reshape(-1)
        off += m.size
    return flat.reshape(NCORES, L0_ROWS, L0_COLS), info


def _unpack_weights_bf16(res, info):
    flat = np.concatenate([np.asarray(res[c]["w_out"]).reshape(-1)
                           for c in range(NCORES)])
    return {k: flat[off:off + int(np.prod(sh))].reshape(sh)
            for k, (off, sh) in info.items()}


def _host_weight_layouts(w16):
    g = {}
    W = w16["inprojT"].reshape(32, 128, 128, 128)
    g["wi_slab"] = _c(W.transpose(2, 1, 0, 3).reshape(128, 128, H))
    Wb = w16["baseT"].reshape(32, 128, 32, 128)
    wb = Wb.transpose(2, 1, 0, 3).reshape(32, 128, H)
    btail = w16["bflat"].reshape(128, 32, 128).transpose(1, 0, 2)
    g["wb_slab"] = _c(np.concatenate([wb, btail], axis=2))
    Af = w16["afT"].reshape(32, 128, 128)
    g["af_slab"] = _c(Af.transpose(1, 0, 2).reshape(128, H))
    g["bflat"] = w16["bflat"]
    Rt = w16["rtT"].reshape(32, 128, 8)
    g["rt_slab"] = _c(Rt.transpose(1, 0, 2).reshape(128, 256))
    g["xp_tiles"] = _c(w16["xpT"].reshape(64, 128, 288))
    Dt = w16["dtpT"].reshape(2, 128, 64, 128)
    g["dtp_slab"] = _c(Dt.transpose(2, 1, 0, 3).reshape(64, 128, 256))
    Wo = w16["outprojT"].reshape(NCORES, 8, 128, 32, 128)
    g["wo_slab"] = _c(Wo.transpose(0, 3, 2, 1, 4)
                      .reshape(NCORES, 32, 128, 8 * 128))
    return g



def alloc_ln_ctx(nc, es, c):
    sb = lambda n_, s, d: es.enter_context(nc.sbuf_tensor(n_, s, d))
    c.sq = sb("sq_a", [128, 512], BF16)
    c.sq2 = sb("sq_b", [128, 512], BF16)
    c.ln_tmp = sb("ln_tmp", [128, 512], BF16)
    c.mub = sb("mub", [128, 512], BF16)
    c.rstdb = sb("rstdb", [128, 512], BF16)
    c.row_mu = sb("row_mu", [1, 512], F32)
    c.row_ex2 = sb("row_ex2", [1, 512], F32)
    c.row_var = sb("row_var", [1, 512], F32)
    c.row_rstd = sb("row_rstd", [1, 512], F32)
    c.row_mu16 = sb("row_mu16", [1, 512], BF16)
    c.row_rstd16 = sb("row_rstd16", [1, 512], BF16)
    c.ones128 = sb("ones128", [128, 1], BF16)
    c.ones1 = sb("ones1", [1, 128], BF16)


def emit_ln_stats(p, c, tile_fn, key_fn):
    psA, psB = c.ps_stat
    with p.group("tensor") as g:
        for k in range(32):
            g.op(lambda E, k=k: E.matmul(psA[:], c.ones128[:], tile_fn(k),
                                         start=(k == 0), stop=(k == 31)),
                 reads=[key_fn(k), "ones"], writes=["psA"])
    for k in range(32):
        sqb = c.sq if k % 2 == 0 else c.sq2
        sqk = "sq" if k % 2 == 0 else "sq2"
        p.op("scalar", lambda E, k=k, sqb=sqb: E.activation(
            out=sqb[:], in_=tile_fn(k), func=AF.Square),
            reads=[key_fn(k)], writes=[sqk])
        with p.group("tensor") as g:
            g.op(lambda E, k=k, sqb=sqb: E.matmul(psB[:], c.ones128[:], sqb[:],
                                         start=(k == 0), stop=(k == 31)),
                 reads=[sqk, "ones"], writes=["psB"])
    p.op("vector", lambda E: E.tensor_scalar(
        out=c.row_mu[:], in0=psA[:], scalar1=1.0 / H, scalar2=None,
        op0=ALU.mult), reads=["psA"], writes=["row_mu"])
    p.op("vector", lambda E: E.tensor_scalar(
        out=c.row_ex2[:], in0=psB[:], scalar1=1.0 / H, scalar2=None,
        op0=ALU.mult), reads=["psB"], writes=["row_ex2"])
    p.op("vector", lambda E: E.scalar_tensor_tensor(
        out=c.row_var[:], in0=c.row_mu[:], scalar=-1.0, in1=c.row_mu[:],
        op0=ALU.mult, op1=ALU.mult), reads=["row_mu"], writes=["row_var"])
    p.op("vector", lambda E: E.tensor_tensor(
        out=c.row_var[:], in0=c.row_ex2[:], in1=c.row_var[:], op=ALU.add),
        reads=["row_ex2", "row_var"], writes=["row_var"])
    p.op("vector", lambda E: E.tensor_scalar(
        out=c.row_var[:], in0=c.row_var[:], scalar1=EPS_LN, scalar2=None,
        op0=ALU.add), reads=["row_var"], writes=["row_var"])
    p.op("scalar", lambda E: E.activation(
        out=c.row_rstd[:], in_=c.row_var[:], func=AF.Ln),
        reads=["row_var"], writes=["row_rstd"])
    p.op("scalar", lambda E: E.activation(
        out=c.row_rstd[:], in_=c.row_rstd[:], func=AF.Exp, scale=-0.5),
        reads=["row_rstd"], writes=["row_rstd"])
    p.op("vector", lambda E: E.tensor_copy(out=c.row_mu16[:], in_=c.row_mu[:]),
         reads=["row_mu"], writes=["row_mu16"])
    p.op("vector", lambda E: E.tensor_copy(out=c.row_rstd16[:], in_=c.row_rstd[:]),
         reads=["row_rstd"], writes=["row_rstd16"])
    for src, skey, dst, dkey in ((c.row_mu16, "row_mu16", c.mub, "mub"),
                                 (c.row_rstd16, "row_rstd16", c.rstdb, "rstdb")):
        with p.group("tensor") as g:
            g.op(lambda E, src=src: E.matmul(c.ps_bcast[:], c.ones1[:], src[:],
                                             start=True, stop=True),
                 reads=[skey, "ones1"], writes=["ps_bcast"])
        p.op("scalar", lambda E, dst=dst: E.activation(
            out=dst[:], in_=c.ps_bcast[:], func=AF.Copy),
            reads=["ps_bcast"], writes=[dkey])


def emit_ln_apply(p, c, in_ap, in_key, out_ap, out_key):
    p.op("vector", lambda E, in_ap=in_ap: E.tensor_tensor(
        out=c.ln_tmp[:], in0=in_ap, in1=c.mub[:], op=ALU.subtract),
        reads=[in_key, "mub"], writes=["ln_tmp"])
    p.op("vector", lambda E, out_ap=out_ap: E.tensor_tensor(
        out=out_ap, in0=c.ln_tmp[:], in1=c.rstdb[:], op=ALU.mult),
        reads=["ln_tmp", "rstdb"], writes=[out_key])


def alloc_ffn_ctx(nc, es, c):
    sb = lambda n_, s, d: es.enter_context(nc.sbuf_tensor(n_, s, d))
    c.lg16 = sb("lg16", [8, 512], BF16)
    c.lgT = sb("lgT", [128, 8], F32)
    c.m1 = sb("m1", [128, 1], F32)
    c.m2 = sb("m2", [128, 1], F32)
    c.dcol = sb("dcol", [128, 1], F32)
    c.s1 = sb("s1", [128, 1], F32)
    c.s2 = sb("s2", [128, 1], F32)
    c.mask1 = sb("mask1", [128, 8], F32)
    c.mask2 = sb("mask2", [128, 8], F32)
    c.maskd = sb("maskd", [128, 8], F32)
    c.wtok = sb("wtok", [128, 8], F32)
    c.wtok16 = sb("wtok16", [128, 8], BF16)
    c.wT16 = sb("wT16", [8, 512], BF16)
    c.wexp16 = sb("wexp16", [128, 512], BF16)
    c.hl16 = sb("hl16", [128, 512], BF16)
    c.g16 = sb("g16", [128, 512], BF16)
    c.af_slab = sb("af_slab", [128, H], BF16)
    c.rt_slab = sb("rt_slab", [128, 256], BF16)
    c.eyet = sb("eyet", [128, 128], BF16)
    c.e8t = sb("e8t", [8, 128], BF16)


def emit_ffn_const_loads(p, c, af_in, rt_in, eye_in, e8_in):
    p.dma("sync", c.af_slab[:], af_in, ch="c_af", writes=["af"])
    p.dma("sync", c.rt_slab[:], rt_in, ch="c_rt", writes=["rt"])
    p.dma("sync", c.eyet[:], eye_in, ch="c_eye", writes=["eye"])
    p.dma("sync", c.e8t[:], e8_in, ch="c_e8", writes=["e8"])


def emit_ffn_chunk(p, c, xln_tile, xln_key, evac_cb):
    ps = c.ps
    with p.group("tensor") as g:
        for k in range(32):
            g.op(lambda E, k=k: E.matmul(c.ps_r[:], c.rt_slab[:, k * 8:(k + 1) * 8],
                                         xln_tile(k), start=(k == 0), stop=(k == 31)),
                 reads=[xln_key(k), "rt"], writes=["ps_r"])
    p.op("vector", lambda E: E.tensor_copy(out=c.lg16[:], in_=c.ps_r[:]),
         reads=["ps_r"], writes=["lg16"])
    for q in range(4):
        with p.group("tensor") as g:
            g.op(lambda E, q=q: E.transpose(
                c.ps_t8[:], c.lg16[:, q * 128:(q + 1) * 128], c.eyet[0:8, 0:8]),
                reads=["lg16", "eye"], writes=["ps_t8"])
        p.op("vector", lambda E: E.tensor_copy(out=c.lgT[:], in_=c.ps_t8[:]),
             reads=["ps_t8"], writes=["lgT"])
        p.op("vector", lambda E: E.tensor_reduce(
            out=c.m1[:], in_=c.lgT[:], axis=AX.X, op=ALU.max),
            reads=["lgT"], writes=["m1"])
        p.op("vector", lambda E: E.tensor_scalar(
            out=c.mask1[:], in0=c.lgT[:], scalar1=c.m1[:], scalar2=None,
            op0=ALU.is_equal), reads=["lgT", "m1"], writes=["mask1"])
        p.op("vector", lambda E: E.scalar_tensor_tensor(
            out=c.maskd[:], in0=c.mask1[:], scalar=-1e30, in1=c.lgT[:],
            op0=ALU.mult, op1=ALU.add), reads=["mask1", "lgT"], writes=["maskd"])
        p.op("vector", lambda E: E.tensor_reduce(
            out=c.m2[:], in_=c.maskd[:], axis=AX.X, op=ALU.max),
            reads=["maskd"], writes=["m2"])
        p.op("vector", lambda E: E.tensor_scalar(
            out=c.mask2[:], in0=c.maskd[:], scalar1=c.m2[:], scalar2=None,
            op0=ALU.is_equal), reads=["maskd", "m2"], writes=["mask2"])
        p.op("vector", lambda E: E.tensor_tensor(
            out=c.dcol[:], in0=c.m1[:], in1=c.m2[:], op=ALU.subtract),
            reads=["m1", "m2"], writes=["dcol"])
        p.op("scalar", lambda E: E.activation(
            out=c.s1[:], in_=c.dcol[:], func=AF.Sigmoid),
            reads=["dcol"], writes=["s1"])
        p.op("scalar", lambda E: E.activation(
            out=c.s2[:], in_=c.dcol[:], func=AF.Sigmoid, scale=-1.0),
            reads=["dcol"], writes=["s2"])
        p.op("vector", lambda E: E.tensor_scalar(
            out=c.s1[:], in0=c.s1[:], scalar1=2.0, scalar2=None, op0=ALU.mult),
            reads=["s1"], writes=["s1"])
        p.op("vector", lambda E: E.tensor_scalar(
            out=c.s2[:], in0=c.s2[:], scalar1=2.0, scalar2=None, op0=ALU.mult),
            reads=["s2"], writes=["s2"])
        p.op("vector", lambda E: E.tensor_scalar(
            out=c.wtok[:], in0=c.mask1[:], scalar1=c.s1[:], scalar2=None,
            op0=ALU.mult), reads=["mask1", "s1"], writes=["wtok"])
        p.op("vector", lambda E: E.scalar_tensor_tensor(
            out=c.wtok[:], in0=c.mask2[:], scalar=c.s2[:], in1=c.wtok[:],
            op0=ALU.mult, op1=ALU.add), reads=["mask2", "s2", "wtok"],
            writes=["wtok"])
        p.op("vector", lambda E: E.tensor_copy(out=c.wtok16[:], in_=c.wtok[:]),
             reads=["wtok"], writes=["wtok16"])
        with p.group("tensor") as g:
            g.op(lambda E: E.transpose(c.ps_t128[:], c.wtok16[:], c.eyet[:]),
                 reads=["wtok16", "eye"], writes=["ps_t128"])
        p.op("vector", lambda E, q=q: E.tensor_copy(
            out=c.wT16[:, q * 128:(q + 1) * 128], in_=c.ps_t128[:]),
            reads=["ps_t128"], writes=["wT16"])
    with p.group("tensor") as g:
        g.op(lambda E: E.matmul(ps[0][:], c.e8t[:], c.wT16[:],
                                start=True, stop=True),
             reads=["wT16", "e8"], writes=[("ps", 0)])
    p.op("scalar", lambda E: E.activation(
        out=c.wexp16[:], in_=ps[0][:], func=AF.Copy),
        reads=[("ps", 0)], writes=["wexp16"])
    with p.group("tensor") as g:
        for k in range(32):
            g.op(lambda E, k=k: E.matmul(
                ps[1][:], c.af_slab[:, k * 128:(k + 1) * 128], xln_tile(k),
                start=(k == 0), stop=(k == 31)),
                reads=[xln_key(k), "af"], writes=[("ps", 1)])
    p.op("scalar", lambda E: E.activation(
        out=c.hl16[:], in_=ps[1][:], func=AF.Copy),
        reads=[("ps", 1)], writes=["hl16"])
    p.op("vector", lambda E: E.tensor_tensor(
        out=c.g16[:], in0=c.hl16[:], in1=c.wexp16[:], op=ALU.mult),
        reads=["hl16", "wexp16"], writes=["g16"])
    for o in range(32):
        s = o % 2
        bank = o % 2
        p.dma("scalar", c.slab[s][:], c.wb_hbm[o], ch=f"slab_{s}",
              writes=[("slab", s)])
        with p.group("tensor") as g:
            for k in range(32):
                g.op(lambda E, k=k, s=s, bank=bank: E.matmul(
                    ps[bank][:], c.slab[s][:, k * 128:(k + 1) * 128],
                    xln_tile(k), start=(k == 0), stop=False),
                    reads=[xln_key(k), ("slab", s)], writes=[("ps", bank)])
            g.op(lambda E, s=s, bank=bank: E.matmul(
                ps[bank][:], c.slab[s][:, H:H + 128], c.g16[:],
                start=False, stop=True),
                reads=["g16", ("slab", s)], writes=[("ps", bank)])
        evac_cb(o, ps[bank], ("ps", bank))



def build_L1():
    nc = bass.Bass("TRN2", target_bir_lowering=False)
    x0T = nc.dram_tensor("x0T", [H, TC], F32, kind="ExternalInput").ap()
    wi_hbm = nc.dram_tensor("wi_hbm", [128, 128, H], BF16, kind="ExternalInput").ap()
    wb_hbm = nc.dram_tensor("wb_hbm", [32, 128, H + 128], BF16, kind="ExternalInput").ap()
    af_in = nc.dram_tensor("af_in", [128, H], BF16, kind="ExternalInput").ap()
    rt_in = nc.dram_tensor("rt_in", [128, 256], BF16, kind="ExternalInput").ap()
    eye_in = nc.dram_tensor("eye_in", [128, 128], BF16, kind="ExternalInput").ap()
    e8_in = nc.dram_tensor("e8_in", [8, 128], BF16, kind="ExternalInput").ap()
    ones_c = nc.dram_tensor("ones_c", [128, 1], BF16, kind="ExternalInput").ap()
    ones_r = nc.dram_tensor("ones_r", [1, 128], BF16, kind="ExternalInput").ap()
    bhalf_in = nc.dram_tensor("bhalf_in", [128, 32], F32, kind="ExternalInput").ap()

    y1T = nc.dram_tensor("y1T", [H, TC], F32, kind="ExternalOutput").ap()
    xpre_o = nc.dram_tensor("xpre_o", [DIN, TC], BF16, kind="ExternalOutput").ap()
    z_o = nc.dram_tensor("z_o", [DIN, TC], BF16, kind="ExternalOutput").ap()

    with contextlib.ExitStack() as es:
        sb = lambda n_, s, d: es.enter_context(nc.sbuf_tensor(n_, s, d))
        c = Ctx()
        XA = sb("XA", [128, 32 * 512], BF16)
        YB = sb("YB", [128, 32 * 512], BF16)
        XLN2 = sb("XLN2", [128, 32 * 1024], BF16)
        c.slab = [sb(f"slab{i}", [128, H + 128], BF16) for i in range(2)]
        x0f = [sb(f"x0f_{i}", [128, 512], F32) for i in range(2)]
        evf = [sb(f"evf_{i}", [128, 512], F32) for i in range(2)]
        ev16 = [sb(f"ev16_{i}", [128, 512], BF16) for i in range(4)]
        bh = sb("bh", [128, 32], F32)
        alloc_ln_ctx(nc, es, c)
        alloc_ffn_ctx(nc, es, c)
        c.ps = [es.enter_context(nc.psum_tensor(f"ps{i}", [128, 512], F32))
                for i in range(2)]
        c.ps_stat = (es.enter_context(nc.psum_tensor("psA", [1, 512], F32)),
                     es.enter_context(nc.psum_tensor("psB", [1, 512], F32)))
        c.ps_bcast = es.enter_context(nc.psum_tensor("psbc", [128, 512], F32))
        c.ps_r = es.enter_context(nc.psum_tensor("ps_r", [8, 512], F32))
        c.ps_t8 = es.enter_context(nc.psum_tensor("ps_t8", [128, 8], BF16))
        c.ps_t128 = es.enter_context(nc.psum_tensor("ps_t128", [8, 128], BF16))
        c.wb_hbm = wb_hbm

        p = Prog(nc)
        emit_ffn_const_loads(p, c, af_in, rt_in, eye_in, e8_in)
        p.dma("sync", c.ones128[:], ones_c, ch="c_o1", writes=["ones"])
        p.dma("sync", c.ones1[:], ones_r, ch="c_o2", writes=["ones1"])
        p.dma("sync", bh[:], bhalf_in, ch="c_bh", writes=["bh"])
        p.op("vector", lambda E: E.tensor_scalar(
            out=bh[:], in0=bh[:], scalar1=0.5, scalar2=None, op0=ALU.mult),
            reads=["bh"], writes=["bh"])

        xa = lambda k: XA[:, k * 512:(k + 1) * 512]
        yb = lambda k: YB[:, k * 512:(k + 1) * 512]

        for half in range(2):
            for tc2 in range(2):
                t0 = half * 1024 + tc2 * 512
                for k in range(32):
                    s = k % 2
                    p.dma("sync", x0f[s][:], x0T[k * 128:(k + 1) * 128, t0:t0 + 512],
                          ch=f"x0_{s}", writes=[("x0f", s)])
                    p.op("scalar", lambda E, k=k, s=s: E.activation(
                        out=xa(k), in_=x0f[s][:], func=AF.Copy),
                        reads=[("x0f", s)], writes=[("xa", k)])
                emit_ln_stats(p, c, xa, lambda k: ("xa", k))
                for k in range(32):
                    emit_ln_apply(p, c, xa(k), ("xa", k), xa(k), ("xa", k))

                def evac1(o, ps_ap, ps_key, t0=t0):
                    s = o % 2
                    p.op("scalar", lambda E, o=o, s=s, ps_ap=ps_ap: E.activation(
                        out=evf[s][:], in_=ps_ap[:], func=AF.Identity, scale=0.5,
                        bias=bh[:, o:o + 1]),
                        reads=[ps_key, "bh"], writes=[("evf", s)])
                    p.dma("sync", x0f[s][:], x0T[o * 128:(o + 1) * 128, t0:t0 + 512],
                          ch=f"x0_{s}", writes=[("x0f", s)])
                    p.op("vector", lambda E, s=s: E.tensor_tensor(
                        out=evf[s][:], in0=evf[s][:], in1=x0f[s][:], op=ALU.add),
                        reads=[("evf", s), ("x0f", s)], writes=[("evf", s)])
                    p.dma("gpsimd", y1T[o * 128:(o + 1) * 128, t0:t0 + 512],
                          evf[s][:], ch=f"y1_{s}", reads=[("evf", s)])
                    p.op("scalar", lambda E, o=o, s=s: E.activation(
                        out=yb(o), in_=evf[s][:], func=AF.Copy),
                        reads=[("evf", s)], writes=[("yb", o)])

                emit_ffn_chunk(p, c, xa, lambda k: ("xa", k), evac1)
                emit_ln_stats(p, c, yb, lambda k: ("yb", k))
                for k in range(32):
                    emit_ln_apply(
                        p, c, yb(k), ("yb", k),
                        XLN2[:, k * 1024 + tc2 * 512:k * 1024 + tc2 * 512 + 512],
                        ("xln2", k, tc2))
            for col in range(128):
                s = col % 2
                p.dma("scalar", c.slab[s][:, 0:H], wi_hbm[col], ch=f"slab_{s}",
                      writes=[("slab", s)])
                for tc2 in range(2):
                    t0 = half * 1024 + tc2 * 512
                    bank = (col * 2 + tc2) % 2
                    with p.group("tensor") as g:
                        for k in range(32):
                            g.op(lambda E, k=k, tc2=tc2, bank=bank, s=s:
                                 E.matmul(c.ps[bank][:],
                                          c.slab[s][:, k * 128:(k + 1) * 128],
                                          XLN2[:, k * 1024 + tc2 * 512:
                                               k * 1024 + tc2 * 512 + 512],
                                          start=(k == 0), stop=(k == 31)),
                                 reads=[("slab", s), ("xln2", k, tc2)],
                                 writes=[("ps", bank)])
                    e = (col * 2 + tc2) % 4
                    p.op("scalar", lambda E, bank=bank, e=e: E.activation(
                        out=ev16[e][:], in_=c.ps[bank][:], func=AF.Copy),
                        reads=[("ps", bank)], writes=[("ev16", e)])
                    if col < 64:
                        dst = xpre_o[col * 128:(col + 1) * 128, t0:t0 + 512]
                    else:
                        dst = z_o[(col - 64) * 128:(col - 63) * 128, t0:t0 + 512]
                    p.dma("gpsimd", dst, ev16[e][:], ch=f"xz_{e}",
                          reads=[("ev16", e)])
        p.emit()
    return nc



def build_L2():
    nc = bass.Bass("TRN2", target_bir_lowering=False)
    xpre = nc.dram_tensor("xpre", [CH, L], BF16, kind="ExternalInput").ap()
    xpw = nc.dram_tensor("xpw", [8, 128, 288], BF16, kind="ExternalInput").ap()
    cw = nc.dram_tensor("cw", [8, 128, 4], F32, kind="ExternalInput").ap()
    cb = nc.dram_tensor("cb", [8, 128, 1], F32, kind="ExternalInput").ap()

    xc_o = nc.dram_tensor("xc_o", [CH, L], BF16, kind="ExternalOutput").ap()
    pdt_o = nc.dram_tensor("pdt_o", [256, L], F32, kind="ExternalOutput").ap()
    pB_o = nc.dram_tensor("pB_o", [16, L], F32, kind="ExternalOutput").ap()
    pC_o = nc.dram_tensor("pC_o", [16, L], F32, kind="ExternalOutput").ap()

    CT = 4096
    with contextlib.ExitStack() as es:
        sb = lambda n_, s, d: es.enter_context(nc.sbuf_tensor(n_, s, d))
        xt = [sb(f"xt_{i}", [128, CT + 3], BF16) for i in range(2)]
        acc = sb("acc", [128, CT], F32)
        xc16 = [sb(f"xc16_{d}", [128, CT], BF16) for d in range(8)]
        wct = sb("wct", [128, 32], F32)
        cbt = sb("cbt", [128, 8], F32)
        xpt = sb("xpt", [128, 288 * 8], BF16)
        pev = [sb(f"pev_{i}", [128, 512], F32) for i in range(2)]
        pevs = [sb(f"pevs_{i}", [16, 512], F32) for i in range(2)]
        ps_dt = [es.enter_context(nc.psum_tensor(f"psd{i}", [128, 512], F32))
                 for i in range(2)]
        ps_bc = [es.enter_context(nc.psum_tensor(f"psb{i}", [16, 512], F32))
                 for i in range(2)]

        p = Prog(nc)
        for d in range(8):
            p.dma("sync", wct[:, d * 4:(d + 1) * 4], cw[d], ch="cw",
                  writes=[("cw", d)])
            p.dma("sync", cbt[:, d:d + 1], cb[d], ch="cb", writes=[("cb", d)])
            p.dma("sync", xpt[:, d * 288:(d + 1) * 288], xpw[d], ch="xpw",
                  writes=[("xpw", d)])

        for t in range(L // CT):
            t0 = t * CT
            for d in range(8):
                s = d % 2
                if t == 0:
                    p.op("vector", lambda E, s=s: E.memset(xt[s][:, 0:3], 0.0),
                         writes=[("xt", s)])
                    p.dma("sync", xt[s][:, 3:], xpre[d * 128:(d + 1) * 128, 0:CT],
                          ch=f"xp_{s}", reads=[("xt", s)], writes=[("xt", s)])
                else:
                    p.dma("sync", xt[s][:],
                          xpre[d * 128:(d + 1) * 128, t0 - 3:t0 + CT],
                          ch=f"xp_{s}", writes=[("xt", s)])
                p.op("scalar", lambda E, s=s, d=d: E.activation(
                    out=acc[:], in_=xt[s][:, 0:CT], func=AF.Identity,
                    scale=wct[:, d * 4:d * 4 + 1], bias=cbt[:, d:d + 1]),
                    reads=[("xt", s), ("cw", d), ("cb", d)], writes=["acc"])
                for k in range(1, 4):
                    p.op("vector", lambda E, s=s, d=d, k=k: E.scalar_tensor_tensor(
                        out=acc[:], in0=xt[s][:, k:k + CT],
                        scalar=wct[:, d * 4 + k:d * 4 + k + 1], in1=acc[:],
                        op0=ALU.mult, op1=ALU.add),
                        reads=[("xt", s), ("cw", d), "acc"], writes=["acc"])
                p.op("scalar", lambda E, d=d: E.activation(
                    out=xc16[d][:], in_=acc[:], func=AF.Silu),
                    reads=["acc"], writes=[("xc", d)])
                p.dma("gpsimd", xc_o[d * 128:(d + 1) * 128, t0:t0 + CT],
                      xc16[d][:], ch=f"xco_{d % 4}", reads=[("xc", d)])
            for nq in range(CT // 512):
                n0 = nq * 512
                for half in range(2):
                    bank = half
                    with p.group("tensor") as g:
                        for d in range(8):
                            g.op(lambda E, d=d, n0=n0, half=half, bank=bank:
                                 E.matmul(ps_dt[bank][:],
                                          xpt[:, d * 288 + half * 128:
                                              d * 288 + half * 128 + 128],
                                          xc16[d][:, n0:n0 + 512],
                                          start=(d == 0), stop=(d == 7)),
                                 reads=[("xc", d), ("xpw", d)],
                                 writes=[("psdt", bank)])
                    p.op("vector", lambda E, half=half, bank=bank: E.tensor_copy(
                        out=pev[half][:], in_=ps_dt[bank][:]),
                        reads=[("psdt", bank)], writes=[("pev", half)])
                    p.dma("gpsimd", pdt_o[half * 128:(half + 1) * 128,
                                          t0 + n0:t0 + n0 + 512], pev[half][:],
                          ch=f"pdt_{half}", reads=[("pev", half)])
                for bi, out_t in ((0, pB_o), (1, pC_o)):
                    off = 256 + bi * 16
                    with p.group("tensor") as g:
                        for d in range(8):
                            g.op(lambda E, d=d, off=off, n0=n0, bi=bi: E.matmul(
                                ps_bc[bi][:],
                                xpt[:, d * 288 + off:d * 288 + off + 16],
                                xc16[d][:, n0:n0 + 512],
                                start=(d == 0), stop=(d == 7)),
                                reads=[("xc", d), ("xpw", d)],
                                writes=[("psbc", bi)])
                    p.op("vector", lambda E, bi=bi: E.tensor_copy(
                        out=pevs[bi][:], in_=ps_bc[bi][:]),
                        reads=[("psbc", bi)], writes=[("pevs", bi)])
                    p.dma("gpsimd", out_t[:, t0 + n0:t0 + n0 + 512], pevs[bi][:],
                          ch=f"pbc_{bi}", reads=[("pevs", bi)])
        p.emit()
    return nc



def build_L3():
    nc = bass.Bass("TRN2", target_bir_lowering=False)
    pdt = nc.dram_tensor("pdt", [NCORES, 256, TC], F32, kind="ExternalInput").ap()
    pB = nc.dram_tensor("pB", [NCORES, 16, TC], F32, kind="ExternalInput").ap()
    pC = nc.dram_tensor("pC", [NCORES, 16, TC], F32, kind="ExternalInput").ap()
    dtw = nc.dram_tensor("dtw", [64, 128, 256], BF16, kind="ExternalInput").ap()
    dtb = nc.dram_tensor("dtb", [128, 64], F32, kind="ExternalInput").ap()
    dtln = nc.dram_tensor("dtln", [128, 2], F32, kind="ExternalInput").ap()
    Bln = nc.dram_tensor("Bln", [16, 1], F32, kind="ExternalInput").ap()
    Cln = nc.dram_tensor("Cln", [16, 1], F32, kind="ExternalInput").ap()
    ones_r = nc.dram_tensor("ones_r", [1, 128], BF16, kind="ExternalInput").ap()
    ones_c = nc.dram_tensor("ones_c", [128, 1], BF16, kind="ExternalInput").ap()

    dt_o = nc.dram_tensor("dt_o", [DIN, TC], BF16, kind="ExternalOutput").ap()
    Brms_o = nc.dram_tensor("Brms_o", [16, TC], F32, kind="ExternalOutput").ap()
    Crms_o = nc.dram_tensor("Crms_o", [16, TC], F32, kind="ExternalOutput").ap()

    with contextlib.ExitStack() as es:
        sb = lambda n_, s, d: es.enter_context(nc.sbuf_tensor(n_, s, d))
        ssm = [sb(f"ssm_{i}", [128, TC], F32) for i in range(2)]
        ssm16 = sb("ssm16", [128, 512], BF16)
        bc = [sb(f"bc_{i}", [16, TC], F32) for i in range(2)]
        dtr16 = [sb(f"dtr16_{i}", [128, TC], BF16) for i in range(2)]
        acc = sb("acc3", [128, 512], F32)
        tmp = sb("tmp3", [128, 512], F32)
        accs = sb("accs", [16, 512], F32)
        tmps = sb("tmps", [16, 512], F32)
        sq16 = sb("sq16", [128, 512], BF16)
        row = sb("row3", [1, 512], F32)
        row16 = sb("row16", [1, 512], BF16)
        rden16 = sb("rden16", [128, 512], BF16)
        rdens = sb("rdens", [16, 512], F32)
        dslab = [sb(f"dslab_{i}", [128, 256], BF16) for i in range(2)]
        dtbt = sb("dtbt", [128, 64], F32)
        dtlnt = sb("dtlnt", [128, 2], F32)
        blnt = sb("blnt", [16, 1], F32)
        clnt = sb("clnt", [16, 1], F32)
        onesr = sb("onesr", [1, 128], BF16)
        onesc = sb("onesc", [128, 1], BF16)
        r_ = sb("r_", [128, 512], F32)
        ab = sb("ab", [128, 512], F32)
        dt16t = [sb(f"dt16t_{i}", [128, 512], BF16) for i in range(2)]
        ps = [es.enter_context(nc.psum_tensor(f"ps3{i}", [128, 512], F32))
              for i in range(2)]
        psr = es.enter_context(nc.psum_tensor("psr3", [1, 512], F32))
        psb = es.enter_context(nc.psum_tensor("psb3", [128, 512], F32))
        psbs = es.enter_context(nc.psum_tensor("psbs", [16, 512], F32))

        p = Prog(nc)
        p.dma("sync", dtbt[:], dtb, ch="c0", writes=["dtb"])
        p.dma("sync", dtlnt[:], dtln, ch="c1", writes=["dtln"])
        p.dma("sync", blnt[:], Bln, ch="c2", writes=["bln"])
        p.dma("sync", clnt[:], Cln, ch="c3", writes=["cln"])
        p.dma("sync", onesr[:], ones_r, ch="c4", writes=["onesr"])
        p.dma("sync", onesc[:], ones_c, ch="c5", writes=["onesc"])

        NQ = TC // 512
        for half in range(2):
            for tq in range(NQ):
                for cc in range(NCORES):
                    dst, key = (acc, "acc") if cc == 0 else (tmp, "tmp")
                    p.dma("sync", dst[:],
                          pdt[cc, half * 128:(half + 1) * 128,
                              tq * 512:(tq + 1) * 512],
                          ch=f"pl_{cc % 2}", writes=[key])
                    if cc > 0:
                        p.op("vector", lambda E: E.tensor_tensor(
                            out=acc[:], in0=acc[:], in1=tmp[:], op=ALU.add),
                            reads=["acc", "tmp"], writes=["acc"])
                p.op("vector", lambda E, half=half, tq=tq: E.tensor_copy(
                    out=ssm[half][:, tq * 512:(tq + 1) * 512], in_=acc[:]),
                    reads=["acc"], writes=[("ssm", half, tq)])
        for bi in range(2):
            src = pB if bi == 0 else pC
            for tq in range(NQ):
                for cc in range(NCORES):
                    dst, key = (accs, "accs") if cc == 0 else (tmps, "tmps")
                    p.dma("sync", dst[:], src[cc, :, tq * 512:(tq + 1) * 512],
                          ch=f"plb_{cc % 2}", writes=[key])
                    if cc > 0:
                        p.op("vector", lambda E: E.tensor_tensor(
                            out=accs[:], in0=accs[:], in1=tmps[:], op=ALU.add),
                            reads=["accs", "tmps"], writes=["accs"])
                p.op("vector", lambda E, bi=bi, tq=tq: E.tensor_copy(
                    out=bc[bi][:, tq * 512:(tq + 1) * 512], in_=accs[:]),
                    reads=["accs"], writes=[("bc", bi, tq)])

        for tq in range(NQ):
            t0 = tq * 512
            for half in range(2):
                p.op("scalar", lambda E, half=half, t0=t0: E.activation(
                    out=sq16[:], in_=ssm[half][:, t0:t0 + 512], func=AF.Square),
                    reads=[("ssm", half, tq)], writes=["sq16"])
                with p.group("tensor") as g:
                    g.op(lambda E, half=half: E.matmul(
                        psr[:], onesc[:], sq16[:],
                        start=(half == 0), stop=(half == 1)),
                        reads=["sq16", "onesc"], writes=["psr"])
            p.op("vector", lambda E: E.tensor_scalar(
                out=row[:], in0=psr[:], scalar1=1.0 / 256, scalar2=None,
                op0=ALU.mult), reads=["psr"], writes=["row"])
            p.op("vector", lambda E: E.tensor_scalar(
                out=row[:], in0=row[:], scalar1=EPS_RMS, scalar2=None,
                op0=ALU.add), reads=["row"], writes=["row"])
            p.op("scalar", lambda E: E.activation(
                out=row[:], in_=row[:], func=AF.Ln),
                reads=["row"], writes=["row"])
            p.op("scalar", lambda E: E.activation(
                out=row[:], in_=row[:], func=AF.Exp, scale=-0.5),
                reads=["row"], writes=["row"])
            p.op("vector", lambda E: E.tensor_copy(out=row16[:], in_=row[:]),
                 reads=["row"], writes=["row16"])
            with p.group("tensor") as g:
                g.op(lambda E: E.matmul(psb[:], onesr[:], row16[:],
                                        start=True, stop=True),
                     reads=["row16", "onesr"], writes=["psbb"])
            p.op("scalar", lambda E: E.activation(
                out=rden16[:], in_=psb[:], func=AF.Copy),
                reads=["psbb"], writes=["rden16"])
            for half in range(2):
                p.op("scalar", lambda E, half=half, t0=t0: E.activation(
                    out=ssm16[:], in_=ssm[half][:, t0:t0 + 512],
                    func=AF.Identity, scale=dtlnt[:, half:half + 1]),
                    reads=[("ssm", half, tq), "dtln"], writes=["ssm16"])
                p.op("vector", lambda E, half=half, t0=t0: E.tensor_tensor(
                    out=dtr16[half][:, t0:t0 + 512], in0=ssm16[:],
                    in1=rden16[:], op=ALU.mult),
                    reads=["ssm16", "rden16"], writes=[("dtr", half, tq)])

        for bi in range(2):
            lnt, lnk = (blnt, "bln") if bi == 0 else (clnt, "cln")
            out_t = Brms_o if bi == 0 else Crms_o
            for tq in range(NQ):
                t0 = tq * 512
                p.op("scalar", lambda E, bi=bi, t0=t0: E.activation(
                    out=sq16[0:16, :], in_=bc[bi][:, t0:t0 + 512],
                    func=AF.Square), reads=[("bc", bi, tq)], writes=["sqs"])
                with p.group("tensor") as g:
                    g.op(lambda E: E.matmul(psr[:], onesc[0:16, :],
                                            sq16[0:16, :], start=True, stop=True),
                         reads=["sqs", "onesc"], writes=["psr"])
                p.op("vector", lambda E: E.tensor_scalar(
                    out=row[:], in0=psr[:], scalar1=1.0 / 16, scalar2=None,
                    op0=ALU.mult), reads=["psr"], writes=["row"])
                p.op("vector", lambda E: E.tensor_scalar(
                    out=row[:], in0=row[:], scalar1=EPS_RMS, scalar2=None,
                    op0=ALU.add), reads=["row"], writes=["row"])
                p.op("scalar", lambda E: E.activation(
                    out=row[:], in_=row[:], func=AF.Ln),
                    reads=["row"], writes=["row"])
                p.op("scalar", lambda E: E.activation(
                    out=row[:], in_=row[:], func=AF.Exp, scale=-0.5),
                    reads=["row"], writes=["row"])
                p.op("vector", lambda E: E.tensor_copy(out=row16[:], in_=row[:]),
                     reads=["row"], writes=["row16"])
                with p.group("tensor") as g:
                    g.op(lambda E: E.matmul(psbs[:], onesr[0:1, 0:16], row16[:],
                                            start=True, stop=True),
                         reads=["row16", "onesr"], writes=["psbs"])
                p.op("vector", lambda E: E.tensor_copy(
                    out=rdens[:], in_=psbs[:]), reads=["psbs"], writes=["rdens"])
                p.op("scalar", lambda E, bi=bi, t0=t0, lnt=lnt, lnk=lnk:
                     E.activation(out=accs[:], in_=bc[bi][:, t0:t0 + 512],
                                  func=AF.Identity, scale=lnt[:]),
                     reads=[("bc", bi, tq), lnk], writes=["accs"])
                p.op("vector", lambda E: E.tensor_tensor(
                    out=accs[:], in0=accs[:], in1=rdens[:], op=ALU.mult),
                    reads=["accs", "rdens"], writes=["accs"])
                p.dma("gpsimd", out_t[:, t0:t0 + 512], accs[:],
                      ch=f"bco_{bi}", reads=["accs"])

        for cht in range(64):
            s = cht % 2
            p.dma("scalar", dslab[s][:], dtw[cht], ch=f"ds_{s}",
                  writes=[("dslab", s)])
            for tq in range(NQ):
                t0 = tq * 512
                bank = (cht * NQ + tq) % 2
                with p.group("tensor") as g:
                    for k in range(2):
                        g.op(lambda E, s=s, k=k, t0=t0, bank=bank, tq=tq:
                             E.matmul(ps[bank][:],
                                      dslab[s][:, k * 128:(k + 1) * 128],
                                      dtr16[k][:, t0:t0 + 512],
                                      start=(k == 0), stop=(k == 1)),
                             reads=[("dslab", s), ("dtr", k, tq)],
                             writes=[("psd", bank)])
                p.op("scalar", lambda E, cht=cht, bank=bank: E.activation(
                    out=r_[:], in_=ps[bank][:], func=AF.Relu,
                    bias=dtbt[:, cht:cht + 1]),
                    reads=[("psd", bank), "dtb"], writes=["r_"])
                p.op("scalar", lambda E, cht=cht, bank=bank: E.activation(
                    out=ab[:], in_=ps[bank][:], func=AF.Abs,
                    bias=dtbt[:, cht:cht + 1]),
                    reads=[("psd", bank), "dtb"], writes=["ab"])
                p.op("scalar", lambda E: E.activation(
                    out=ab[:], in_=ab[:], func=AF.Exp, scale=-1.0),
                    reads=["ab"], writes=["ab"])
                p.op("vector", lambda E: E.tensor_scalar(
                    out=ab[:], in0=ab[:], scalar1=1.0, scalar2=None,
                    op0=ALU.add), reads=["ab"], writes=["ab"])
                p.op("scalar", lambda E: E.activation(
                    out=ab[:], in_=ab[:], func=AF.Ln),
                    reads=["ab"], writes=["ab"])
                e = tq % 2
                p.op("vector", lambda E, e=e: E.tensor_tensor(
                    out=dt16t[e][:], in0=r_[:], in1=ab[:], op=ALU.add),
                    reads=["r_", "ab"], writes=[("dt16t", e)])
                p.dma("gpsimd", dt_o[cht * 128:(cht + 1) * 128, t0:t0 + 512],
                      dt16t[e][:], ch=f"dto_{e}", reads=[("dt16t", e)])
        p.emit()
    return nc



def build_L4():
    nc = bass.Bass("TRN2", target_bir_lowering=False)
    dt_in = nc.dram_tensor("dt_in", [CH, L], BF16, kind="ExternalInput").ap()
    xc_in = nc.dram_tensor("xc_in", [CH, L], BF16, kind="ExternalInput").ap()
    z_in = nc.dram_tensor("z_in", [CH, L], BF16, kind="ExternalInput").ap()
    Br = nc.dram_tensor("Br", [16, L], F32, kind="ExternalInput").ap()
    Cr = nc.dram_tensor("Cr", [16, L], F32, kind="ExternalInput").ap()
    Alog = nc.dram_tensor("Alog", [8, 128, 16], F32, kind="ExternalInput").ap()
    Dp = nc.dram_tensor("Dp", [8, 128, 1], F32, kind="ExternalInput").ap()
    wo = nc.dram_tensor("wo", [32, 128, 1024], BF16, kind="ExternalInput").ap()
    ones_r = nc.dram_tensor("ones_r", [1, 128], BF16, kind="ExternalInput").ap()

    po = nc.dram_tensor("po", [H, L], BF16, kind="ExternalOutput").ap()

    TQ = 1024
    NQ = L // TQ
    with contextlib.ExitStack() as es:
        sb = lambda n_, s, d: es.enter_context(nc.sbuf_tensor(n_, s, d))
        dtt = [sb(f"dtt_{i}", [128, TQ], BF16) for i in range(2)]
        xct = [sb(f"xct_{i}", [128, TQ], BF16) for i in range(2)]
        zt = [sb(f"zt_{i}", [128, TQ], BF16) for i in range(2)]
        dtx = sb("dtx", [128, TQ], BF16)
        a_t = [sb(f"a_t{i}", [128, TQ], F32) for i in range(2)]
        u_t = sb("u_t", [128, TQ], BF16)
        h_t = [sb(f"h_t{i}", [128, TQ], F32) for i in range(2)]
        v_t = sb("v_t", [128, TQ], F32)
        ys = sb("ys", [128, TQ], F32)
        gsil = sb("gsil", [128, TQ], F32)
        y16 = [sb(f"y16_{d}", [128, TQ], BF16) for d in range(8)]
        carry = [sb(f"carry_{d}", [128, 16], F32) for d in range(8)]
        At = sb("At", [128, 128], F32)
        Dt = sb("Dt", [128, 8], F32)
        brow = sb("brow", [1, TQ], F32)
        brow16 = sb("brow16", [1, TQ], BF16)
        Bb = sb("Bb", [128, 16 * TQ], BF16)
        Cb = sb("Cb", [128, 16 * TQ], BF16)
        onesr = sb("onesr", [1, 128], BF16)
        wslab = [sb(f"wos_{i}", [128, 1024], BF16) for i in range(2)]
        pev = [sb(f"pev_{i}", [128, 512], BF16) for i in range(4)]
        ps = [es.enter_context(nc.psum_tensor(f"ps4{i}", [128, 512], F32))
              for i in range(4)]
        psb = es.enter_context(nc.psum_tensor("psb4", [128, 512], F32))

        p = Prog(nc)
        p.dma("sync", onesr[:], ones_r, ch="c0", writes=["onesr"])
        for d in range(8):
            p.dma("sync", At[:, d * 16:(d + 1) * 16], Alog[d], ch="cA",
                  writes=[("A", d)])
            p.dma("sync", Dt[:, d:d + 1], Dp[d], ch="cD", writes=[("D", d)])
        for d in range(8):
            p.op("scalar", lambda E, d=d: E.activation(
                out=At[:, d * 16:(d + 1) * 16], in_=At[:, d * 16:(d + 1) * 16],
                func=AF.Exp), reads=[("A", d)], writes=[("A", d)])
            p.op("vector", lambda E, d=d: E.tensor_scalar(
                out=At[:, d * 16:(d + 1) * 16],
                in0=At[:, d * 16:(d + 1) * 16], scalar1=-1.0, scalar2=None,
                op0=ALU.mult), reads=[("A", d)], writes=[("A", d)])
            p.op("vector", lambda E, d=d: E.memset(carry[d][:], 0.0),
                 writes=[("carry", d)])

        for q in range(NQ):
            t0 = q * TQ
            for n in range(16):
                for src, dstb, tag in ((Br, Bb, "Bb"), (Cr, Cb, "Cb")):
                    p.dma("sync", brow[:], src[n:n + 1, t0:t0 + TQ], ch="br",
                          writes=["brow"])
                    p.op("vector", lambda E: E.tensor_copy(
                        out=brow16[:], in_=brow[:]),
                        reads=["brow"], writes=["brow16"])
                    for n2 in range(TQ // 512):
                        with p.group("tensor") as g:
                            g.op(lambda E, n2=n2: E.matmul(
                                psb[:], onesr[:],
                                brow16[:, n2 * 512:(n2 + 1) * 512],
                                start=True, stop=True),
                                reads=["brow16", "onesr"], writes=["psbb"])
                        p.op("scalar", lambda E, n=n, n2=n2, dstb=dstb:
                             E.activation(
                                 out=dstb[:, n * TQ + n2 * 512:
                                          n * TQ + (n2 + 1) * 512],
                                 in_=psb[:], func=AF.Copy),
                             reads=["psbb"], writes=[(tag, n)])
            for d in range(8):
                s = d % 2
                p.dma("sync", dtt[s][:], dt_in[d * 128:(d + 1) * 128, t0:t0 + TQ],
                      ch=f"dt_{s}", writes=[("dtt", s)])
                p.dma("sync", xct[s][:], xc_in[d * 128:(d + 1) * 128, t0:t0 + TQ],
                      ch=f"xc_{s}", writes=[("xct", s)])
                p.dma("sync", zt[s][:], z_in[d * 128:(d + 1) * 128, t0:t0 + TQ],
                      ch=f"z_{s}", writes=[("zt", s)])
                p.op("vector", lambda E, s=s: E.tensor_tensor(
                    out=dtx[:], in0=dtt[s][:], in1=xct[s][:], op=ALU.mult),
                    reads=[("dtt", s), ("xct", s)], writes=["dtx"])
                for n in range(16):
                    b = n % 2
                    p.op("scalar", lambda E, s=s, d=d, n=n, b=b: E.activation(
                        out=a_t[b][:], in_=dtt[s][:], func=AF.Exp,
                        scale=At[:, d * 16 + n:d * 16 + n + 1]),
                        reads=[("dtt", s), ("A", d)], writes=[("a_t", b)])
                    p.op("vector", lambda E, n=n: E.tensor_tensor(
                        out=u_t[:], in0=dtx[:], in1=Bb[:, n * TQ:(n + 1) * TQ],
                        op=ALU.mult), reads=["dtx", ("Bb", n)], writes=["u_t"])
                    p.op("vector", lambda E, d=d, n=n, b=b: E.tensor_tensor_scan(
                        out=h_t[b][:], data0=a_t[b][:], data1=u_t[:],
                        initial=carry[d][:, n:n + 1], op0=ALU.mult, op1=ALU.add),
                        reads=[("a_t", b), "u_t", ("carry", d)],
                        writes=[("h_t", b)])
                    p.op("vector", lambda E, d=d, n=n, b=b: E.tensor_copy(
                        out=carry[d][:, n:n + 1], in_=h_t[b][:, TQ - 1:TQ]),
                        reads=[("h_t", b)], writes=[("carry", d)])
                    if n == 0:
                        p.op("gpsimd", lambda E, n=n, b=b: E.tensor_tensor(
                            out=ys[:], in0=h_t[b][:],
                            in1=Cb[:, n * TQ:(n + 1) * TQ],
                            op=ALU.mult), reads=[("h_t", b), ("Cb", n)],
                            writes=["ys"])
                    else:
                        p.op("gpsimd", lambda E, n=n, b=b: E.tensor_tensor(
                            out=v_t[:], in0=h_t[b][:],
                            in1=Cb[:, n * TQ:(n + 1) * TQ],
                            op=ALU.mult), reads=[("h_t", b), ("Cb", n)],
                            writes=["v_t"])
                        p.op("gpsimd", lambda E: E.tensor_tensor(
                            out=ys[:], in0=ys[:], in1=v_t[:], op=ALU.add),
                            reads=["ys", "v_t"], writes=["ys"])
                p.op("scalar", lambda E, s=s: E.activation(
                    out=gsil[:], in_=zt[s][:], func=AF.Silu),
                    reads=[("zt", s)], writes=["gsil"])
                p.op("vector", lambda E, s=s, d=d: E.scalar_tensor_tensor(
                    out=ys[:], in0=xct[s][:], scalar=Dt[:, d:d + 1], in1=ys[:],
                    op0=ALU.mult, op1=ALU.add),
                    reads=[("xct", s), ("D", d), "ys"], writes=["ys"])
                p.op("gpsimd", lambda E, d=d: E.tensor_tensor(
                    out=y16[d][:], in0=ys[:], in1=gsil[:], op=ALU.mult),
                    reads=["ys", "gsil"], writes=[("y16", d)])
            for ht in range(32):
                s = ht % 2
                p.dma("scalar", wslab[s][:], wo[ht], ch=f"wo_{s}",
                      writes=[("wos", s)])
                for n2 in range(TQ // 512):
                    n0 = n2 * 512
                    bank = (ht * 2 + n2) % 4
                    with p.group("tensor") as g:
                        for d in range(8):
                            g.op(lambda E, d=d, s=s, n0=n0, bank=bank: E.matmul(
                                ps[bank][:], wslab[s][:, d * 128:(d + 1) * 128],
                                y16[d][:, n0:n0 + 512],
                                start=(d == 0), stop=(d == 7)),
                                reads=[("wos", s), ("y16", d)],
                                writes=[("ps", bank)])
                    e = (ht * 2 + n2) % 4
                    p.op("scalar", lambda E, bank=bank, e=e: E.activation(
                        out=pev[e][:], in_=ps[bank][:], func=AF.Copy),
                        reads=[("ps", bank)], writes=[("pev", e)])
                    p.dma("gpsimd", po[ht * 128:(ht + 1) * 128,
                                       t0 + n0:t0 + n0 + 512], pev[e][:],
                          ch=f"po_{e}", reads=[("pev", e)])
        p.emit()
    return nc



def build_L5():
    nc = bass.Bass("TRN2", target_bir_lowering=False)
    pin = nc.dram_tensor("pin", [NCORES, H, TC], BF16, kind="ExternalInput").ap()
    y1T = nc.dram_tensor("y1T", [H, TC], F32, kind="ExternalInput").ap()
    wb_hbm = nc.dram_tensor("wb_hbm", [32, 128, H + 128], BF16, kind="ExternalInput").ap()
    af_in = nc.dram_tensor("af_in", [128, H], BF16, kind="ExternalInput").ap()
    rt_in = nc.dram_tensor("rt_in", [128, 256], BF16, kind="ExternalInput").ap()
    eye_in = nc.dram_tensor("eye_in", [128, 128], BF16, kind="ExternalInput").ap()
    e8_in = nc.dram_tensor("e8_in", [8, 128], BF16, kind="ExternalInput").ap()
    ones_c = nc.dram_tensor("ones_c", [128, 1], BF16, kind="ExternalInput").ap()
    ones_r = nc.dram_tensor("ones_r", [1, 128], BF16, kind="ExternalInput").ap()
    bhalf_in = nc.dram_tensor("bhalf_in", [128, 32], F32, kind="ExternalInput").ap()

    outT = nc.dram_tensor("outT", [H, TC], F32, kind="ExternalOutput").ap()
    y2_scr = nc.dram_tensor("y2_scr", [H, TC], F32, kind="Internal").ap()

    with contextlib.ExitStack() as es:
        sb = lambda n_, s, d: es.enter_context(nc.sbuf_tensor(n_, s, d))
        c = Ctx()
        YB = sb("YB", [128, 32 * 512], BF16)
        XLN3 = sb("XLN3", [128, 32 * 1024], BF16)
        c.slab = [sb(f"slab{i}", [128, H + 128], BF16) for i in range(2)]
        acc = sb("acc5", [128, 512], F32)
        t16 = sb("t16_5", [128, 512], BF16)
        yf = [sb(f"yf_{i}", [128, 512], F32) for i in range(2)]
        evf = [sb(f"evf_{i}", [128, 512], F32) for i in range(2)]
        bh = sb("bh", [128, 32], F32)
        alloc_ln_ctx(nc, es, c)
        alloc_ffn_ctx(nc, es, c)
        c.ps = [es.enter_context(nc.psum_tensor(f"ps{i}", [128, 512], F32))
                for i in range(2)]
        c.ps_stat = (es.enter_context(nc.psum_tensor("psA", [1, 512], F32)),
                     es.enter_context(nc.psum_tensor("psB", [1, 512], F32)))
        c.ps_bcast = es.enter_context(nc.psum_tensor("psbc", [128, 512], F32))
        c.ps_r = es.enter_context(nc.psum_tensor("ps_r", [8, 512], F32))
        c.ps_t8 = es.enter_context(nc.psum_tensor("ps_t8", [128, 8], BF16))
        c.ps_t128 = es.enter_context(nc.psum_tensor("ps_t128", [8, 128], BF16))
        c.wb_hbm = wb_hbm

        p = Prog(nc)
        emit_ffn_const_loads(p, c, af_in, rt_in, eye_in, e8_in)
        p.dma("sync", c.ones128[:], ones_c, ch="c_o1", writes=["ones"])
        p.dma("sync", c.ones1[:], ones_r, ch="c_o2", writes=["ones1"])
        p.dma("sync", bh[:], bhalf_in, ch="c_bh", writes=["bh"])
        p.op("vector", lambda E: E.tensor_scalar(
            out=bh[:], in0=bh[:], scalar1=0.5, scalar2=None, op0=ALU.mult),
            reads=["bh"], writes=["bh"])

        yb = lambda k: YB[:, k * 512:(k + 1) * 512]

        for half in range(2):
          for tc2 in range(2):
            tc = half * 2 + tc2
            t0 = tc * 512
            for k in range(32):
                s = k % 2
                for cc in range(NCORES):
                    dst, key = (t16, "t16") if True else (None, None)
                    p.dma("sync", t16[:], pin[cc, k * 128:(k + 1) * 128,
                                              t0:t0 + 512],
                          ch=f"p5_{cc % 2}", writes=["t16"])
                    if cc == 0:
                        p.op("vector", lambda E: E.tensor_copy(
                            out=acc[:], in_=t16[:]), reads=["t16"],
                            writes=["acc"])
                    else:
                        p.op("vector", lambda E: E.tensor_tensor(
                            out=acc[:], in0=acc[:], in1=t16[:], op=ALU.add),
                            reads=["acc", "t16"], writes=["acc"])
                p.dma("sync", yf[s][:], y1T[k * 128:(k + 1) * 128, t0:t0 + 512],
                      ch=f"y1_{s}", writes=[("yf", s)])
                p.op("vector", lambda E, s=s: E.tensor_tensor(
                    out=yf[s][:], in0=yf[s][:], in1=acc[:], op=ALU.add),
                    reads=[("yf", s), "acc"], writes=[("yf", s)])
                p.dma("gpsimd", y2_scr[k * 128:(k + 1) * 128, t0:t0 + 512],
                      yf[s][:], ch=f"y2_{s}", reads=[("yf", s)])
                p.op("scalar", lambda E, k=k, s=s: E.activation(
                    out=yb(k), in_=yf[s][:], func=AF.Copy),
                    reads=[("yf", s)], writes=[("yb", k)])
            emit_ln_stats(p, c, yb, lambda k: ("yb", k))
            for k in range(32):
                emit_ln_apply(p, c, yb(k), ("yb", k),
                              XLN3[:, k * 1024 + tc2 * 512:k * 1024 + tc2 * 512 + 512],
                              ("xln3", k, tc2))

          for tc2 in range(2):
            tc = half * 2 + tc2
            t0 = tc * 512
            if True:
              def evac2(o, ps_ap, ps_key, t0=t0):
                s = o % 2
                p.op("scalar", lambda E, o=o, s=s, ps_ap=ps_ap: E.activation(
                    out=evf[s][:], in_=ps_ap[:], func=AF.Identity, scale=0.5,
                    bias=bh[:, o:o + 1]),
                    reads=[ps_key, "bh"], writes=[("evf", s)])
                p.dma("sync", yf[s][:], y2_scr[o * 128:(o + 1) * 128, t0:t0 + 512],
                      ch=f"y2r_{s}", writes=[("yf", s)])
                p.op("vector", lambda E, s=s: E.tensor_tensor(
                    out=evf[s][:], in0=evf[s][:], in1=yf[s][:], op=ALU.add),
                    reads=[("evf", s), ("yf", s)], writes=[("evf", s)])
                p.dma("gpsimd", outT[o * 128:(o + 1) * 128, t0:t0 + 512],
                      evf[s][:], ch=f"ot_{s}", reads=[("evf", s)])

              emit_ffn_chunk(p, c,
                             lambda k, tc2=tc2: XLN3[:, k * 1024 + tc2 * 512:k * 1024 + tc2 * 512 + 512],
                             lambda k, tc2=tc2: ("xln3", k, tc2), evac2)
        p.emit()
    return nc



_CACHE = {}


def _get(name, builder):
    if name not in _CACHE:
        _CACHE[name] = builder()
    return _CACHE[name]


def kernel(**inputs):
    cores = list(range(NCORES))
    x0 = np.asarray(inputs["L0_attn_hidden_states"], np.float32).reshape(L, H)
    x0T = _c(x0.T)

    slabs, info = _pack_weights_fp32(inputs)
    nc0 = _get("L0", build_L0)
    r0 = run_bass_kernel_spmd(nc0, [{"w_in": _c(slabs[cc])} for cc in cores],
                              core_ids=cores).results
    w16 = _unpack_weights_bf16(r0, info)
    g = _host_weight_layouts(w16)

    eye = np.eye(128, dtype=BF)
    e8 = np.zeros((8, 128), BF)
    for e_ in range(8):
        e8[e_, e_ * 16:(e_ + 1) * 16] = 1
    ones_c = np.ones((128, 1), BF)
    ones_r = np.ones((1, 128), BF)
    b_base = np.asarray(inputs["L0_ffn_b_base"], np.float32)
    braw = _c(b_base.reshape(32, 128).T)

    nc1 = _get("L1", build_L1)
    in1 = []
    for cc in cores:
        in1.append(dict(
            x0T=_c(x0T[:, cc * TC:(cc + 1) * TC]),
            wi_hbm=g["wi_slab"], wb_hbm=g["wb_slab"], af_in=g["af_slab"],
            rt_in=g["rt_slab"], eye_in=eye, e8_in=e8,
            ones_c=ones_c, ones_r=ones_r, bhalf_in=braw,
        ))
    r1 = run_bass_kernel_spmd(nc1, in1, core_ids=cores).results

    y1T_full = np.concatenate([r1[cc]["y1T"] for cc in cores], axis=1)
    xpre_full = np.concatenate([r1[cc]["xpre_o"] for cc in cores], axis=1)
    z_full = np.concatenate([r1[cc]["z_o"] for cc in cores], axis=1)

    conv_w = np.asarray(inputs["L0_attn_conv_w"], np.float32)
    conv_b = np.asarray(inputs["L0_attn_conv_b"], np.float32)
    nc2 = _get("L2", build_L2)
    in2 = []
    for cc in cores:
        in2.append(dict(
            xpre=_c(xpre_full[cc * CH:(cc + 1) * CH, :]),
            xpw=_c(g["xp_tiles"][cc * 8:(cc + 1) * 8]),
            cw=_c(conv_w[cc * CH:(cc + 1) * CH].reshape(8, 128, 4)),
            cb=_c(conv_b[cc * CH:(cc + 1) * CH].reshape(8, 128, 1)),
        ))
    r2 = run_bass_kernel_spmd(nc2, in2, core_ids=cores).results
    xc_full = np.concatenate([r2[cc]["xc_o"] for cc in cores], axis=0)
    pdt_all = np.stack([r2[cc]["pdt_o"] for cc in cores])
    pB_all = np.stack([r2[cc]["pB_o"] for cc in cores])
    pC_all = np.stack([r2[cc]["pC_o"] for cc in cores])

    dt_bias = np.asarray(inputs["L0_attn_dt_bias"], np.float32)
    dt_ln = np.asarray(inputs["L0_attn_dt_ln_w"], np.float32)
    B_ln = np.asarray(inputs["L0_attn_B_ln_w"], np.float32)
    C_ln = np.asarray(inputs["L0_attn_C_ln_w"], np.float32)
    nc3 = _get("L3", build_L3)
    in3 = []
    for cc in cores:
        sl = slice(cc * TC, (cc + 1) * TC)
        in3.append(dict(
            pdt=_c(pdt_all[:, :, sl]), pB=_c(pB_all[:, :, sl]),
            pC=_c(pC_all[:, :, sl]),
            dtw=g["dtp_slab"], dtb=_c(dt_bias.reshape(64, 128).T),
            dtln=_c(dt_ln.reshape(2, 128).T), Bln=_c(B_ln.reshape(16, 1)),
            Cln=_c(C_ln.reshape(16, 1)), ones_r=ones_r, ones_c=ones_c,
        ))
    r3 = run_bass_kernel_spmd(nc3, in3, core_ids=cores).results
    dt_full = np.concatenate([r3[cc]["dt_o"] for cc in cores], axis=1)
    B_full = np.concatenate([r3[cc]["Brms_o"] for cc in cores], axis=1)
    C_full = np.concatenate([r3[cc]["Crms_o"] for cc in cores], axis=1)

    A_log = np.asarray(inputs["L0_attn_A_log"], np.float32)
    D_par = np.asarray(inputs["L0_attn_D_param"], np.float32)
    nc4 = _get("L4", build_L4)
    in4 = []
    for cc in cores:
        sl = slice(cc * CH, (cc + 1) * CH)
        in4.append(dict(
            dt_in=_c(dt_full[sl]), xc_in=_c(xc_full[sl]),
            z_in=_c(z_full[sl]), Br=B_full, Cr=C_full,
            Alog=_c(A_log[sl].reshape(8, 128, 16)),
            Dp=_c(D_par[sl].reshape(8, 128, 1)),
            wo=_c(g["wo_slab"][cc]), ones_r=ones_r,
        ))
    r4 = run_bass_kernel_spmd(nc4, in4, core_ids=cores).results
    po_all = np.stack([r4[cc]["po"] for cc in cores])

    nc5 = _get("L5", build_L5)
    in5 = []
    for cc in cores:
        sl = slice(cc * TC, (cc + 1) * TC)
        in5.append(dict(
            pin=_c(po_all[:, :, sl]),
            y1T=_c(y1T_full[:, sl]),
            wb_hbm=g["wb_slab"], af_in=g["af_slab"],
            rt_in=g["rt_slab"], eye_in=eye, e8_in=e8, ones_c=ones_c,
            ones_r=ones_r, bhalf_in=braw,
        ))
    r5 = run_bass_kernel_spmd(nc5, in5, core_ids=cores).results
    outT = np.concatenate([r5[cc]["outT"] for cc in cores], axis=1)
    return _c(outT.T).reshape(1, L, H)
